# version 02
# baseline (speedup 1.0000x reference)
import math
import os
import numpy as np
import concourse.bass as bass
import concourse.mybir as mybir
from concourse.bass_utils import run_bass_kernel_spmd
from concourse.alu_op_type import AluOpType as OP

F32 = mybir.dt.float32
BF16 = mybir.dt.bfloat16
FP16 = mybir.dt.float16
I32 = mybir.dt.int32
AF = mybir.ActivationFunctionType

B = 2
C = 1024
H = 16
NPAIR = H // 2
HID = 3072
DQ = 512
DKV = 512
SCALE_A = 1.0 / 8.0
SCALE_B = 1.0 / math.sqrt(H + DKV + DQ)
EPS = 1e-5
DEQ = float(np.float32(1.0 / 127.0))
SKIP_T = 75.0

S_G = 8192
N_SUB = 4
KSKIP = set(os.environ.get("KSKIP", "").split(","))


def _alibi_slopes(n):
    start = 2.0 ** (-(2.0 ** -(math.log2(n) - 3)))
    return [start * start ** i for i in range(n)]

SLOPES = _alibi_slopes(H)


def geom(s_g):
    TOK = s_g // 4
    CHW = min(512, TOK)
    return dict(S=s_g, TOK=TOK, NQT=TOK // 128, QCH=TOK // CHW, CHW=CHW,
                CHT=CHW // 128, KT=s_g // 128,
                W=5248 + min(s_g, 8448), OFF=min(s_g, 8448) + 640)


def classify(g):
    CHT, KT, QCH = g["CHT"], g["KT"], g["QCH"]
    out = []
    for h in range(H):
        s = SLOPES[h]
        c_skip = SKIP_T / s
        sat_lo = 8.0 / s
        byh = []
        for gk in range(KT):
            byj = []
            for j in range(QCH):
                a = 0
                for t in range(CHT):
                    base = 4 * (CHT * j + t)
                    rhi = 128 * (base + 3) + 127 - 128 * gk
                    if rhi < -c_skip:
                        a = t + 1
                    else:
                        break
                b = CHT
                for t in range(CHT - 1, a - 1, -1):
                    base = 4 * (CHT * j + t)
                    rlo = 128 * base - (128 * gk + 127)
                    if rlo >= sat_lo:
                        b = t
                    else:
                        break
                byj.append((a, b))
            byh.append(byj)
        out.append(byh)
    return out


def _split_perm():
    idx = np.zeros(C, np.int64)
    for comp in range(2):
        for p in range(C // 2):
            idx[comp * (C // 2) + p] = 2 * p + comp
    return idx


def host_prep(inp, g):
    s_g = g["S"]
    x = np.asarray(inp["L0_attn_a_x"], np.float32)
    perm = _split_perm()
    A = lambda k: np.asarray(inp[k], np.float32)
    col = lambda v: np.ascontiguousarray(np.asarray(v, np.float32).reshape(-1, 1))
    com = {
        "wqkv_t": np.ascontiguousarray(A("L0_attn_a_in_proj_weight").T),
        "bqkv": col(A("L0_attn_a_in_proj_bias")),
        "woa_t": np.ascontiguousarray(A("L0_attn_a_out_proj_weight").T),
        "boa": col(A("L0_attn_a_out_proj_bias")),
        "wdq": np.ascontiguousarray(A("L0_attn_b_Wd_q_w")),
        "bdq": col(A("L0_attn_b_Wd_q_b")),
        "wuq_p": np.ascontiguousarray(A("L0_attn_b_Wu_q_w")[:, perm]),
        "buq_p": col(A("L0_attn_b_Wu_q_b")[perm]),
        "wqr_p": np.ascontiguousarray(A("L0_attn_b_Wq_r_w")[:, perm]),
        "bqr_p": col(A("L0_attn_b_Wq_r_b")[perm]),
        "wkr_p": np.ascontiguousarray(A("L0_attn_b_Wk_r_w")[:, perm]),
        "bkr_p": col(A("L0_attn_b_Wk_r_b")[perm]),
        "wdkv": np.ascontiguousarray(A("L0_attn_b_Wd_kv_w")),
        "bdkv": col(A("L0_attn_b_Wd_kv_b")),
        "wuk_p": np.ascontiguousarray(A("L0_attn_b_Wu_k_w")[:, perm]),
        "wuv": np.ascontiguousarray(A("L0_attn_b_Wu_v_w")),
        "buv": col(A("L0_attn_b_Wu_v_b")),
        "wob": np.ascontiguousarray(A("L0_attn_b_Wo_w")),
        "bob": col(A("L0_attn_b_Wo_b")),
        "wg_t": np.ascontiguousarray(np.asarray(inp["L0_ffn_w_gate"], np.int32).T),
        "wu_t": np.ascontiguousarray(np.asarray(inp["L0_ffn_w_up"], np.int32).T),
        "wd_t": np.ascontiguousarray(np.asarray(inp["L0_ffn_w_down"], np.int32).T),
        "identin": np.eye(128, dtype=np.float32),
    }
    per_core = []
    for core in range(8):
        b, r = core // 4, core % 4
        d = dict(com)
        xb = x[b].reshape(s_g // 128, 128, C)
        d["x_shard"] = np.ascontiguousarray(xb[r::4])
        gt = np.arange(r, s_g // 128, 4)
        pos = (gt[:, None] * 128 + np.arange(128)[None, :]).reshape(-1).astype(np.float64)
        ids = np.arange(C // 2, dtype=np.float64)
        theta = (10000.0 ** (-2.0 * ids)) / (C // 2)
        rr = pos[None, :] * theta[:, None]
        d["cosq"] = np.ascontiguousarray(np.cos(rr).astype(np.float32))
        d["sinq"] = np.ascontiguousarray(np.sin(rr).astype(np.float32))
        offv = np.zeros((128, H), np.float32)
        for h in range(H):
            offv[:, h] = 8.0 + SLOPES[h] * (g["OFF"] - 128.0 * r)
        d["offv"] = offv
        per_core.append(d)
    return per_core


class Ctx:
    def __init__(self, nc, cur, eng, T):
        self.nc, self.cur, self.eng, self.T = nc, cur, eng, T
        self.n = dict(mm=0, ex=0, dv=0, pl=0, ls=0, lg=0, cc=0, so=0, l1=0, l2=0, lh=0)

    def mm(self, out, lhsT, rhs, start=True, stop=True, tp=None):
        self.n["mm"] += 1
        if self.cur == "tensor":
            self.eng.matmul(out, lhsT, rhs, start=start, stop=stop,
                            tile_position=tp, skip_group_check=True).then_inc(self.T["s_mm"], 1)

    def ldw(self, w, tp=None):
        if self.cur == "tensor":
            try:
                dt = w.tensor.dtype
            except AttributeError:
                dt = None
            if dt == F32:
                return
            self.eng.ldweights(w, tile_position=tp)

    def trans(self, out, in_):
        self.n["mm"] += 1
        if self.cur == "tensor":
            self.eng.transpose(out, in_, self.T["ident"][:]).then_inc(self.T["s_mm"], 1)

    def _selfw(self, eng, sem):
        if self.cur == eng and self.n[sem] > 0:
            self.eng.wait_ge(self.T["s_" + sem], self.n[sem])

    def act(self, out, in_, func, bias=0.0, scale=1.0):
        self._selfw("scalar", "ex")
        self.n["ex"] += 1
        if self.cur == "scalar":
            self.eng.activation(out, in_, func, bias=bias, scale=scale).then_inc(self.T["s_ex"], 1)

    def tt(self, out, in0, in1, op):
        self._selfw("vector", "dv")
        self.n["dv"] += 1
        if self.cur == "vector":
            self.eng.tensor_tensor(out, in0, in1, op).then_inc(self.T["s_dv"], 1)

    def ts(self, out, in0, s1, op0, s2=None, op1=None):
        self._selfw("vector", "dv")
        self.n["dv"] += 1
        if self.cur == "vector":
            if s2 is None:
                self.eng.tensor_scalar(out, in0, s1, None, op0).then_inc(self.T["s_dv"], 1)
            else:
                self.eng.tensor_scalar(out, in0, s1, s2, op0, op1).then_inc(self.T["s_dv"], 1)

    def cp(self, out, in_):
        self._selfw("vector", "dv")
        self.n["dv"] += 1
        if self.cur == "vector":
            self.eng.tensor_copy(out, in_).then_inc(self.T["s_dv"], 1)

    def memset(self, ap, v):
        self._selfw("vector", "dv")
        self.n["dv"] += 1
        if self.cur == "vector":
            self.eng.memset(ap, v).then_inc(self.T["s_dv"], 1)

    def red(self, out, in_, op):
        self._selfw("vector", "dv")
        self.n["dv"] += 1
        if self.cur == "vector":
            self.eng.tensor_reduce(out, in_, mybir.AxisListType.X, op).then_inc(self.T["s_dv"], 1)

    def iota(self, out, pattern, base, chmul):
        self._selfw("gpsimd", "pl")
        self.n["pl"] += 1
        if self.cur == "gpsimd":
            self.eng.iota(out, pattern, base=base, channel_multiplier=chmul,
                          allow_small_or_imprecise_dtypes=True).then_inc(self.T["s_pl"], 1)

    def _dma(self, qeng, sem, out, in_):
        n = self.n[sem]
        if self.cur == qeng:
            if n > 0:
                self.eng.wait_ge(self.T["s_" + sem], 16 * n)
            self.eng.dma_start(out=out, in_=in_).then_inc(self.T["s_" + sem], 16)
        self.n[sem] = n + 1
        return (sem, n + 1)

    def dls(self, out, in_, ch=0):
        return self._dma("sync", "ls" if ch == 0 else f"l{ch}", out, in_)

    def dlg(self, out, in_, ch=0):
        return self._dma("gpsimd", "lg" if ch == 0 else "lh", out, in_)

    def dso(self, out, in_):
        return self._dma("sync", "so", out, in_)

    def waitt(self, eng, tok):
        if tok is not None and self.cur == eng and tok[1] > 0:
            self.eng.wait_ge(self.T["s_" + tok[0]], 16 * tok[1])

    def coll(self, kind, ins, outs):
        n = self.n["cc"]
        if self.cur == "gpsimd":
            if n > 0:
                self.eng.wait_ge(self.T["s_cc"], n)
            self.eng.collective_compute(kind, OP.bypass,
                                        replica_groups=[[0, 1, 2, 3], [4, 5, 6, 7]],
                                        ins=ins, outs=outs).then_inc(self.T["s_cc"], 1)
        self.n["cc"] = n + 1

    def wait(self, eng, sem, val):
        if self.cur == eng and val > 0:
            mult = 16 if sem in ("ls", "lg", "so") else 1
            self.eng.wait_ge(self.T["s_" + sem], mult * val)

    def wall(self, sem, val):
        for e in ("sync", "vector", "tensor", "scalar", "gpsimd"):
            self.wait(e, sem, val)


def build_program(g, cls, n_sub):
    nc = bass.Bass("TRN2", target_bir_lowering=False)
    TOK, NQT = g["TOK"], g["NQT"]
    W = g["W"]
    T = {}

    def EIN(name, shape, dt):
        T[name] = nc.dram_tensor(name, shape, dt, kind="ExternalInput")

    EIN("x_shard", [NQT, 128, C], F32)
    EIN("wqkv_t", [C, 3 * C], F32); EIN("bqkv", [3 * C, 1], F32)
    EIN("woa_t", [C, C], F32); EIN("boa", [C, 1], F32)
    EIN("wdq", [C, DQ], F32); EIN("bdq", [DQ, 1], F32)
    EIN("wuq_p", [DQ, C], F32); EIN("buq_p", [C, 1], F32)
    EIN("wqr_p", [DQ, C], F32); EIN("bqr_p", [C, 1], F32)
    EIN("wkr_p", [C, C], F32); EIN("bkr_p", [C, 1], F32)
    EIN("wdkv", [C, DKV], F32); EIN("bdkv", [DKV, 1], F32)
    EIN("wuk_p", [DKV, C], F32)
    EIN("wuv", [DKV, C], F32); EIN("buv", [C, 1], F32)
    EIN("wob", [C, C], F32); EIN("bob", [C, 1], F32)
    EIN("wg_t", [C, HID], I32); EIN("wu_t", [C, HID], I32); EIN("wd_t", [HID, C], I32)
    EIN("cosq", [C // 2, TOK], F32); EIN("sinq", [C // 2, TOK], F32)
    EIN("offv", [128, H], F32)
    EIN("identin", [128, 128], F32)
    T["out_shard"] = nc.dram_tensor("out_shard", [NQT, 128, C], F32, kind="ExternalOutput")

    T["bandD"] = nc.dram_tensor("bandD", [H, 128, W], FP16)
    T["wg_bf"] = nc.dram_tensor("wg_bf", [C, HID], BF16)
    T["wu_bf"] = nc.dram_tensor("wu_bf", [C, HID], BF16)
    T["wd_bf"] = nc.dram_tensor("wd_bf", [HID, C], BF16)
    T["kT_loc"] = nc.dram_tensor("kT_loc", [C, TOK], BF16)
    T["v_loc"] = nc.dram_tensor("v_loc", [NPAIR, TOK, 128], BF16)
    T["q_asm"] = nc.dram_tensor("q_asm", [C, TOK], BF16)
    for p in range(NPAIR):
        T[f"kg{p}"] = nc.dram_tensor(f"kg{p}", [4 * 128, TOK], BF16)
        T[f"vg{p}"] = nc.dram_tensor(f"vg{p}", [4 * TOK, 128], BF16)
    T["kst_in"] = nc.dram_tensor("kst_in", [1, 16], F32)
    T["kst_out"] = nc.dram_tensor("kst_out", [4, 16], F32)

    from contextlib import ExitStack
    es = ExitStack()
    SB = lambda n_, s_, d_: es.enter_context(nc.sbuf_tensor(n_, s_, d_))
    PSt = lambda n_, s_, d_: es.enter_context(nc.psum_tensor(n_, s_, d_))

    T["xT"] = SB("xT", [128, 8 * TOK], F32)
    T["poolA"] = SB("poolA", [128, 8 * TOK], BF16)
    T["poolB"] = SB("poolB", [128, max(8 * TOK, (HID // 128) * min(512, TOK))], BF16)
    T["qt"] = SB("qt", [128, 8 * TOK], BF16)
    T["scrF"] = SB("scrF", [128, 3072], F32)
    T["aT"] = SB("aT", [128, 4096], BF16)
    T["pS"] = SB("pS", [128, 2048], BF16)
    T["csT"] = SB("csT", [128, 1024], F32)
    T["bsl"] = SB("bsl", [128, 2048], FP16)
    T["bias"] = SB("biasv", [128, 16], F32)
    T["offs"] = SB("offs", [128, H], F32)
    T["ident"] = SB("ident", [128, 128], F32)
    T["onesf"] = SB("onesf", [128, 2], F32)
    T["onesb"] = SB("onesb", [128, 2], BF16)
    T["orow"] = SB("orow", [1, 128], F32)
    T["wsA"] = SB("wsA", [128, 4096], BF16)
    T["srow"] = SB("srow", [1, 512], F32)
    T["st2"] = SB("st2", [1, 512], F32)
    T["st3"] = SB("st3", [1, 576], F32)
    T["st4"] = SB("st4", [1, 64], F32)

    T["P0"] = PSt("P0", [128, 1024], F32)
    T["P1"] = PSt("P1", [128, 1024], F32)
    T["P2"] = PSt("P2", [128, 1024], F32)
    T["P3"] = PSt("P3", [128, 1024], F32)

    for s in ("s_mm", "s_ex", "s_dv", "s_pl", "s_ls", "s_lg", "s_cc", "s_so", "s_l1", "s_l2", "s_lh"):
        T[s] = es.enter_context(nc.semaphore(s))

    with es, nc.Block() as block:
        @block.sync
        def _(e):
            program(Ctx(nc, "sync", e, T), T, g, cls, n_sub)

        @block.vector
        def _(e):
            program(Ctx(nc, "vector", e, T), T, g, cls, n_sub)

        @block.tensor
        def _(e):
            program(Ctx(nc, "tensor", e, T), T, g, cls, n_sub)

        @block.scalar
        def _(e):
            program(Ctx(nc, "scalar", e, T), T, g, cls, n_sub)

        @block.gpsimd
        def _(e):
            program(Ctx(nc, "gpsimd", e, T), T, g, cls, n_sub)

    return nc


def program(c, T, g, cls, n_sub):
    TOK, NQT, QCH, CHW, CHT, KT = g["TOK"], g["NQT"], g["QCH"], g["CHW"], g["CHT"], g["KT"]
    W, OFFB, S_g = g["W"], g["OFF"], g["S"]
    xT, poolA, poolB, qt = T["xT"], T["poolA"], T["poolB"], T["qt"]
    scrF, aT, pS, csT, bsl = T["scrF"], T["aT"], T["pS"], T["csT"], T["bsl"]
    P0, P1, P2, P3 = T["P0"], T["P1"], T["P2"], T["P3"]
    SPAIR = [P0, P1]

    PB = max(8 * TOK, (HID // 128) * CHW)
    XTc = lambda ci: xT[:, ci * TOK:(ci + 1) * TOK]
    XN = lambda ci: poolA[:, ci * TOK:(ci + 1) * TOK]

    c.dls(T["ident"][:], T["identin"][:, :])
    tok_consts = c.dls(T["offs"][:], T["offv"][:, :])
    c.memset(T["onesf"][:], 1.0)
    c.memset(T["orow"][:], 1.0)
    c.memset(T["onesb"][:], 1.0)
    c.memset(T["st4"][:], 0.0)

    c.waitt("tensor", tok_consts)
    if "entry" in KSKIP:
        for t in range(NQT):
            c.wait("sync", "dv", c.n["dv"])
            c.dls(xT[:, t * 1024:(t + 1) * 1024] if TOK >= 1024 else xT[:, 0:1024], T["x_shard"][t])
    else:
      for t in range(NQT):
        c.wait("sync", "mm", c.n["mm"])
        c.wait("sync", "dv", c.n["dv"])
        tok_x = c.dls(scrF[:, 2048:3072], T["x_shard"][t])
        c.waitt("tensor", tok_x)
        c.wait("tensor", "dv", c.n["dv"])
        for ci in range(8):
            c.trans(P2[:, ci * 128:ci * 128 + 128], scrF[:, 2048 + ci * 128:2048 + (ci + 1) * 128])
        c.wait("vector", "mm", c.n["mm"])
        for ci in range(8):
            c.cp(XTc(ci)[:, t * 128:(t + 1) * 128], P2[:, ci * 128:ci * 128 + 128])

    if "band" in KSKIP:
        nbx = 0
    c.waitt("gpsimd", tok_consts)
    c.waitt("scalar", tok_consts)
    nb = 0 if "band" in KSKIP else (W + 1023) // 1024
    for h in range(H if nb else 0):
        for bi in range(nb):
            w0 = bi * 1024
            wid = min(1024, W - w0)
            c.wait("gpsimd", "ex", c.n["ex"])
            c.iota(scrF[:, 0:wid], [[1, wid]], w0, -1)
            c.wait("scalar", "pl", c.n["pl"])
            c.wait("scalar", "dv", c.n["dv"])
            c.act(scrF[:, 1024:1024 + wid], scrF[:, 0:wid], AF.Relu,
                  bias=T["offs"][:, h:h + 1], scale=-float(SLOPES[h]))
            c.wait("vector", "ex", c.n["ex"])
            c.wait("vector", "ls", c.n["ls"])
            c.ts(bsl[:, 0:wid], scrF[:, 1024:1024 + wid], -1.0, OP.mult, 8.0, OP.add)
            c.wait("sync", "dv", c.n["dv"])
            c.dls(T["bandD"][h, :, w0:w0 + wid], bsl[:, 0:wid])

    wsI = scrF[:, 1792:3072].bitcast(I32)
    for (src_, dst, rows) in (() if "deq" in KSKIP else (("wg_t", "wg_bf", C), ("wu_t", "wu_bf", C), ("wd_t", "wd_bf", HID))):
        ncol = HID if rows == C else C
        for r0 in range(0, rows, 128):
            for c0 in range(0, ncol, 1280):
                cw = min(1280, ncol - c0)
                c.wait("gpsimd", "dv", c.n["dv"])
                tok_w = c.dlg(wsI[:, 0:cw], T[src_][r0:r0 + 128, c0:c0 + cw])
                c.waitt("vector", tok_w)
                c.wait("vector", "ls", c.n["ls"])
                c.ts(T["wsA"][:, 0:cw], wsI[:, 0:cw], DEQ, OP.mult)
                c.wait("sync", "dv", c.n["dv"])
                c.dls(T[dst][r0:r0 + 128, c0:c0 + cw], T["wsA"][:, 0:cw])
    c.wait("vector", "ls", c.n["ls"])

    def layernorm():
        for j in range(QCH):
            q0 = j * CHW
            for ci in range(8):
                c.wait("vector", "mm", c.n["mm"])
                c.tt(scrF[:, 0:CHW], XTc(ci)[:, q0:q0 + CHW], XTc(ci)[:, q0:q0 + CHW], OP.mult)
                c.wait("tensor", "dv", c.n["dv"])
                c.mm(P2[0:1, 0:CHW], T["onesf"][:, 0:1], XTc(ci)[:, q0:q0 + CHW],
                     start=(ci == 0), stop=(ci == 7))
                c.mm(P2[0:1, 512:512 + CHW], T["onesf"][:, 0:1], scrF[:, 0:CHW],
                     start=(ci == 0), stop=(ci == 7))
            c.wait("vector", "mm", c.n["mm"])
            c.ts(T["srow"][0:1, 0:CHW], P2[0:1, 0:CHW], 1.0 / C, OP.mult)
            c.ts(T["st2"][0:1, 0:CHW], P2[0:1, 512:512 + CHW], 1.0 / C, OP.mult)
            c.tt(T["st3"][0:1, 0:CHW], T["srow"][0:1, 0:CHW], T["srow"][0:1, 0:CHW], OP.mult)
            c.tt(T["st2"][0:1, 0:CHW], T["st2"][0:1, 0:CHW], T["st3"][0:1, 0:CHW], OP.subtract)
            c.ts(T["st2"][0:1, 0:CHW], T["st2"][0:1, 0:CHW], EPS, OP.add)
            c.wait("scalar", "dv", c.n["dv"])
            c.act(T["st3"][0:1, 0:CHW], T["st2"][0:1, 0:CHW], AF.Ln, bias=0.0, scale=1.0)
            c.act(T["st3"][0:1, 0:CHW], T["st3"][0:1, 0:CHW], AF.Exp, bias=0.0, scale=-0.5)
            c.wait("vector", "ex", c.n["ex"])
            c.tt(T["st2"][0:1, 0:CHW], T["srow"][0:1, 0:CHW], T["st3"][0:1, 0:CHW], OP.mult)
            c.ts(T["st2"][0:1, 0:CHW], T["st2"][0:1, 0:CHW], -1.0, OP.mult)
            c.wait("tensor", "dv", c.n["dv"])
            c.mm(P3[:, 0:CHW], T["orow"][0:1, :], T["st3"][0:1, 0:CHW])
            c.mm(P3[:, 512:512 + CHW], T["orow"][0:1, :], T["st2"][0:1, 0:CHW])
            c.wait("vector", "mm", c.n["mm"])
            for ci in range(8):
                c.tt(scrF[:, 0:CHW], XTc(ci)[:, q0:q0 + CHW], P3[:, 0:CHW], OP.mult)
                c.tt(XN(ci)[:, q0:q0 + CHW], scrF[:, 0:CHW], P3[:, 512:512 + CHW], OP.add)
            c.wait("tensor", "dv", c.n["dv"])

    def wload(w_ap, col0, ncols, nchunk, dst_off=0):
        pitch = w_ap.shape[1]
        c.wait("gpsimd", "mm", c.n["mm"])
        return c.dlg(T["wsA"][:, dst_off:dst_off + nchunk * ncols],
                     bass.AP(w_ap, col0, [[pitch, 128], [128 * pitch, nchunk], [1, ncols]]))

    def proj_fm(w_ap, cout0, n_cout, dst_write, cin_srcs, pre_co=None):
        ncin = len(cin_srcs)
        for co in range(n_cout // 128):
            if pre_co is not None:
                pre_co(co)
            tok_w = wload(w_ap, cout0 + co * 128, 128, ncin)
            for j in range(QCH):
                pb = SPAIR[(co * QCH + j) % 2]
                c.waitt("tensor", tok_w)
                c.wait("tensor", "dv", c.n["dv"])
                for ki in range(ncin):
                    src, ofs = cin_srcs[ki]
                    c.ldw(T["wsA"][:, ki * 128:(ki + 1) * 128])
                    c.mm(pb[:, 0:CHW], T["wsA"][:, ki * 128:(ki + 1) * 128],
                         src[:, ofs + j * CHW:ofs + j * CHW + CHW],
                         start=(ki == 0), stop=(ki == ncin - 1))
                c.wait("vector", "mm", c.n["mm"])
                dst_write(co, j, pb[:, 0:CHW])

    def xn_srcs():
        return [(poolA, ci * TOK) for ci in range(8)]

    def halfsum_max(src_sb, statcol):
        c.wait("vector", "mm", c.n["mm"])
        c.tt(scrF[:, 0:CHW], src_sb, src_sb, OP.mult)
        c.wait("tensor", "dv", c.n["dv"])
        c.mm(P2[0:1, 0:CHW], T["onesf"][0:64, 0:1], scrF[0:64, 0:CHW], tp=(0, 0))
        c.mm(P2[0:1, 512:512 + CHW], T["onesf"][64:128, 0:1], scrF[64:128, 0:CHW], tp=(64, 0))
        c.wait("vector", "mm", c.n["mm"])
        c.red(T["st2"][0:1, 0:1], P2[0:1, 0:CHW], OP.max)
        c.tt(T["st4"][0:1, statcol:statcol + 1], T["st4"][0:1, statcol:statcol + 1],
             T["st2"][0:1, 0:1], OP.max)
        c.red(T["st2"][0:1, 1:2], P2[0:1, 512:512 + CHW], OP.max)
        c.tt(T["st4"][0:1, statcol:statcol + 1], T["st4"][0:1, statcol:statcol + 1],
             T["st2"][0:1, 1:2], OP.max)
        c.wait("tensor", "dv", c.n["dv"])

    def kstat_and_bound(statk, statq, addcap, mixcol, satcol):
        c.wait("sync", "dv", c.n["dv"])
        c.dls(T["kst_in"][:, :], T["st4"][0:1, 0:16])
        c.wait("gpsimd", "ls", c.n["ls"])
        c.coll("AllGather", [T["kst_in"][:, :].opt()], [T["kst_out"][:, :].opt()])
        c.wall("cc", c.n["cc"])
        c.wait("sync", "dv", c.n["dv"])
        tok_s = c.dls(T["st3"][0:1, 0:64], bass.AP(T["kst_out"], 0, [[64, 1], [1, 64]]))
        c.waitt("vector", tok_s)
        c.red(T["st3"][0:1, 64:65], T["st3"][0:1, 0:64], OP.max)
        c.tt(T["st2"][0:1, 0:1], T["st3"][0:1, 64:65], T["st4"][0:1, statq:statq + 1], OP.mult)
        c.ts(T["st2"][0:1, 0:1], T["st2"][0:1, 0:1], 1e-30, OP.add)
        c.wait("scalar", "dv", c.n["dv"])
        c.act(T["st2"][0:1, 1:2], T["st2"][0:1, 0:1], AF.Ln, bias=0.0, scale=1.0)
        c.act(T["st2"][0:1, 1:2], T["st2"][0:1, 1:2], AF.Exp, bias=0.0, scale=0.5)
        c.wait("vector", "ex", c.n["ex"])
        c.ts(T["st2"][0:1, 2:3], T["st2"][0:1, 1:2], float(addcap), OP.add, -1.0, OP.mult)
        c.ts(T["st2"][0:1, 3:4], T["st2"][0:1, 1:2], -1.0, OP.mult)
        c.wait("tensor", "dv", c.n["dv"])
        c.mm(P2[:, 0:2], T["orow"][0:1, :], T["st2"][0:1, 2:4])
        c.wait("vector", "mm", c.n["mm"])
        c.cp(T["bias"][:, mixcol:mixcol + 1], P2[:, 0:1])
        c.cp(T["bias"][:, satcol:satcol + 1], P2[:, 1:2])
        c.wait("tensor", "dv", c.n["dv"])

    def attention(with_band, mixcol, satcol, vbias_ap, vb0, wo_ap, bo_ap, cc_k, cc_v):
        last_expk = {0: 0, 1: 0}
        last_pvz = {0: 0, 1: 0}
        buf_mm = {0: 0, 1: 0}
        bsl_dv = {0: 0, 1: 0, 2: 0, 3: 0}
        unit = 0
        for j in range(QCH):
            for p in range(NPAIR):
                buf = (j * NPAIR + p) % 2
                hA, hB = 2 * p, 2 * p + 1
                kg, vg = T[f"kg{p}"], T[f"vg{p}"]
                c.wait("sync", "mm", buf_mm[buf])
                c.wait("sync", "cc", max(cc_k[p], cc_v[p]))
                tok_k = tok_v = None
                for r in range(4):
                    tok_k = c.dls(bass.AP(poolA, buf * S_g + 128 * r,
                                          [[8 * TOK, 128], [512, TOK // 128], [1, 128]]),
                                  bass.AP(kg, (128 * r) * TOK,
                                          [[TOK, 128], [128, TOK // 128], [1, 128]]), ch=1)
                    tok_v = c.dls(bass.AP(poolB, buf * S_g + 128 * r,
                                          [[PB, 128], [512, TOK // 128], [1, 128]]),
                                  bass.AP(vg, (TOK * r) * 128,
                                          [[128, 128], [128 * 128, TOK // 128], [1, 128]]), ch=2)

                def KS(r0, nr, c0, w_):
                    return bass.AP(poolA, buf * S_g + r0 * (8 * TOK) + c0,
                                   [[8 * TOK, nr], [1, w_]])

                def VS(r0, nr, c0, w_):
                    return bass.AP(poolB, buf * S_g + r0 * PB + c0,
                                   [[PB, nr], [1, w_]])

                surv = []
                for gk in range(KT):
                    if with_band:
                        aA, bA = cls[hA][gk][j]
                        aB, bB = cls[hB][gk][j]
                    else:
                        aA = bA = aB = bB = 0
                    if aA < CHT or aB < CHT:
                        surv.append((gk, aA, bA, aB, bB))
                first_h = [True, True]
                lastA = max((i for i, s in enumerate(surv) if s[1] < CHT), default=-1)
                lastB = max((i for i, s in enumerate(surv) if s[3] < CHT), default=-1)
                for si, (gk, aA, bA, aB, bB) in enumerate(surv):
                    par = unit % 2
                    sp = SPAIR[par]
                    unit += 1
                    c.waitt("tensor", tok_k)
                    c.waitt("tensor", tok_v)
                    c.wait("tensor", "ex", last_expk[par])
                    for (hh, ah) in ((0, aA), (1, aB)):
                        if ah >= CHT:
                            continue
                        nq = (CHT - ah) * 128
                        q0 = j * CHW + ah * 128
                        rr0 = 0 if hh == 0 else 64
                        qrhs = qt[rr0:rr0 + 64, p * TOK + q0: p * TOK + q0 + nq]
                        for (tp, pr0, klo) in (((rr0, 0), 0, 0), ((rr0, 64), 64, 64)):
                            lhs = KS(rr0, 64, 128 * gk + klo, 64)
                            c.ldw(lhs, tp)
                            c.mm(sp[pr0:pr0 + 64, hh * 512 + ah * 128: hh * 512 + ah * 128 + nq],
                                 lhs, qrhs, tp=tp)
                    ck_s = c.n["mm"]
                    ck_band = c.n["dv"]
                    if with_band:
                        for (hh, h, ah, bh) in ((0, hA, aA, bA), (1, hB, aB, bB)):
                            if bh <= ah:
                                continue
                            nqt_ = CHT - ah
                            m0 = OFFB + 128 * (4 * (CHT * j + ah) - gk)
                            assert 0 <= m0 and m0 + (nqt_ - 1) * 512 + 255 < W, (h, gk, j, m0)
                            slot = (unit % 2) * 2 + hh
                            bslv = bsl[:, slot * 512:slot * 512 + nqt_ * 128]
                            c.wait("gpsimd", "dv", bsl_dv[slot])
                            tok_b = c.dlg(bslv, bass.AP(T["bandD"], (h * 128) * W + m0,
                                                        [[W, 128], [512, nqt_], [1, 128]]), ch=1)
                            c.waitt("vector", tok_b)
                            c.wait("vector", "mm", ck_s)
                            c.tt(sp[:, hh * 512 + ah * 128: hh * 512 + CHW],
                                 sp[:, hh * 512 + ah * 128: hh * 512 + CHW], bslv, OP.add)
                            bsl_dv[slot] = c.n["dv"]
                        ck_band = c.n["dv"]
                    c.wait("scalar", "mm", ck_s)
                    c.wait("scalar", "dv", ck_band)
                    c.wait("scalar", "mm", last_pvz[par])
                    for (hh, ah, bh) in ((0, aA, bA), (1, aB, bB)):
                        if ah >= CHT:
                            continue
                        bc = mixcol if (with_band and bh > ah) else satcol
                        c.act(pS[:, par * 1024 + hh * 512 + ah * 128: par * 1024 + hh * 512 + CHW],
                              sp[:, hh * 512 + ah * 128: hh * 512 + CHW],
                              AF.Exp, bias=T["bias"][:, bc:bc + 1], scale=1.0)
                    ck_e = c.n["ex"]
                    last_expk[par] = ck_e
                    c.wait("tensor", "ex", ck_e)
                    for (tpos, vr0, pvr0, hh) in (((0, 0), 0, 0, 0), ((0, 64), 0, 64, 1),
                                                  ((64, 0), 64, 0, 0), ((64, 64), 64, 64, 1)):
                        ah = aA if hh == 0 else aB
                        if ah >= CHT:
                            continue
                        stop_h = (si == (lastA if hh == 0 else lastB))
                        bank = 0 if vr0 == 0 else 512
                        lhs = VS(vr0, 64, 128 * gk + hh * 64, 64)
                        prhs = pS[vr0:vr0 + 64, par * 1024 + hh * 512 + ah * 128: par * 1024 + hh * 512 + CHW]
                        c.ldw(lhs, tpos)
                        c.mm(P2[pvr0:pvr0 + 64, bank + ah * 128: bank + CHW],
                             lhs, prhs, start=first_h[hh], stop=stop_h, tp=tpos)
                    for (tpos, vr0, zr, hh) in (((0, 0), 0, 0, 0), ((0, 64), 0, 64, 1),
                                                ((64, 0), 64, 0, 0), ((64, 64), 64, 64, 1)):
                        ah = aA if hh == 0 else aB
                        if ah >= CHT:
                            continue
                        stop_h = (si == (lastA if hh == 0 else lastB))
                        bank = 0 if vr0 == 0 else 512
                        prhs = pS[vr0:vr0 + 64, par * 1024 + hh * 512 + ah * 128: par * 1024 + hh * 512 + CHW]
                        c.ldw(T["onesb"][vr0:vr0 + 64, 0:1], tpos)
                        c.mm(P3[zr:zr + 1, bank + ah * 128: bank + CHW],
                             T["onesb"][vr0:vr0 + 64, 0:1], prhs, start=first_h[hh], stop=stop_h, tp=tpos)
                    if aA < CHT:
                        first_h[0] = False
                    if aB < CHT:
                        first_h[1] = False
                    last_pvz[par] = c.n["mm"]
                buf_mm[buf] = c.n["mm"]
                c.wait("vector", "mm", c.n["mm"])
                c.cp(T["st2"][0:1, 0:CHW], P3[0:1, 0:CHW])
                c.tt(T["st2"][0:1, 0:CHW], T["st2"][0:1, 0:CHW], P3[0:1, 512:512 + CHW], OP.add)
                c.cp(T["st3"][0:1, 0:CHW], P3[64:65, 0:CHW])
                c.tt(T["st3"][0:1, 0:CHW], T["st3"][0:1, 0:CHW], P3[64:65, 512:512 + CHW], OP.add)
                c.wait("scalar", "dv", c.n["dv"])
                c.act(T["st2"][0:1, 0:CHW], T["st2"][0:1, 0:CHW], AF.Ln, bias=0.0, scale=1.0)
                c.act(T["st2"][0:1, 0:CHW], T["st2"][0:1, 0:CHW], AF.Exp, bias=0.0, scale=-1.0)
                c.act(T["st3"][0:1, 0:CHW], T["st3"][0:1, 0:CHW], AF.Ln, bias=0.0, scale=1.0)
                c.act(T["st3"][0:1, 0:CHW], T["st3"][0:1, 0:CHW], AF.Exp, bias=0.0, scale=-1.0)
                c.wait("tensor", "ex", c.n["ex"])
                c.mm(P3[0:64, 0:CHW], T["orow"][0:1, 0:64], T["st2"][0:1, 0:CHW], tp=(0, 0))
                c.mm(P3[64:128, 0:CHW], T["orow"][0:1, 0:64], T["st3"][0:1, 0:CHW], tp=(0, 64))
                c.wait("sync", "dv", c.n["dv"])
                tok_vb = c.dls(T["bias"][:, 0:1], vbias_ap[vb0 + p * 128: vb0 + (p + 1) * 128, :])
                c.wait("vector", "mm", c.n["mm"])
                c.waitt("vector", tok_vb)
                c.cp(scrF[:, 0:CHW], P2[:, 512:512 + CHW])
                c.tt(scrF[:, 512:512 + CHW], P2[:, 0:CHW], scrF[:, 0:CHW], OP.add)
                c.tt(scrF[:, 0:CHW], scrF[:, 512:512 + CHW], P3[:, 0:CHW], OP.mult)
                c.ts(aT[:, p * 512:p * 512 + CHW], scrF[:, 0:CHW], T["bias"][:, 0:1], OP.add)
                c.wait("tensor", "dv", c.n["dv"])
            for co in range(8):
                tok_w = wload(wo_ap, co * 128, 128, 8)
                c.wait("sync", "dv", c.n["dv"])
                tok_b = c.dls(T["bias"][:, 4:5], bo_ap[co * 128:(co + 1) * 128, :])
                pb = SPAIR[co % 2]
                c.waitt("tensor", tok_w)
                c.wait("tensor", "ex", max(last_expk[0], last_expk[1]))
                c.wait("tensor", "dv", c.n["dv"])
                for ki in range(8):
                    c.ldw(T["wsA"][:, ki * 128:(ki + 1) * 128])
                    c.mm(pb[:, 0:CHW], T["wsA"][:, ki * 128:(ki + 1) * 128],
                         aT[:, ki * 512:ki * 512 + CHW], start=(ki == 0), stop=(ki == 7))
                c.wait("vector", "mm", c.n["mm"])
                c.waitt("vector", tok_b)
                c.ts(scrF[:, 0:CHW], pb[:, 0:CHW], T["bias"][:, 4:5], OP.add)
                c.tt(XTc(co)[:, j * CHW:(j + 1) * CHW], XTc(co)[:, j * CHW:(j + 1) * CHW],
                     scrF[:, 0:CHW], OP.add)
                c.wait("tensor", "dv", c.n["dv"])

    def ffn(second):
        layernorm()
        for j in range(QCH):
            q0 = j * CHW
            for hc in range(HID // 128):
                tok_g = wload(T["wg_bf"], hc * 128, 128, 8, dst_off=0)
                tok_u = wload(T["wu_bf"], hc * 128, 128, 8, dst_off=1024)
                c.waitt("tensor", tok_g)
                c.waitt("tensor", tok_u)
                c.wait("tensor", "dv", c.n["dv"])
                c.wait("tensor", "ex", c.n["ex"])
                for (bank, pbank) in ((0, P0), (1, P1)):
                    for ki in range(8):
                        wt = T["wsA"][:, bank * 1024 + ki * 128:bank * 1024 + (ki + 1) * 128]
                        c.ldw(wt)
                        c.mm(pbank[:, 0:CHW], wt, XN(ki)[:, q0:q0 + CHW],
                             start=(ki == 0), stop=(ki == 7))
                c.wait("scalar", "mm", c.n["mm"])
                c.wait("scalar", "dv", c.n["dv"])
                c.wait("scalar", "so", c.n["so"])
                c.act(scrF[:, 0:CHW], P0[:, 0:CHW], AF.Silu, bias=0.0, scale=1.0)
                c.wait("vector", "ex", c.n["ex"])
                c.tt(poolB[:, hc * CHW:hc * CHW + CHW], scrF[:, 0:CHW], P1[:, 0:CHW], OP.mult)
                c.wait("tensor", "dv", c.n["dv"])
            if not second:
                for co in range(8):
                    pb = SPAIR[co % 2]
                    tok_w = wload(T["wd_bf"], co * 128, 128, 24)
                    c.waitt("tensor", tok_w)
                    c.wait("tensor", "dv", c.n["dv"])
                    for kk in range(24):
                        c.ldw(T["wsA"][:, kk * 128:kk * 128 + 128])
                        c.mm(pb[:, 0:CHW], T["wsA"][:, kk * 128:kk * 128 + 128],
                             poolB[:, kk * CHW:kk * CHW + CHW],
                             start=(kk == 0), stop=(kk == 23))
                    c.wait("vector", "mm", c.n["mm"])
                    c.tt(XTc(co)[:, q0:q0 + CHW], XTc(co)[:, q0:q0 + CHW], pb[:, 0:CHW], OP.add)
                    c.wait("tensor", "dv", c.n["dv"])
            else:
                for tt_ in range(CHT):
                    t_loc = j * CHT + tt_
                    c.wait("tensor", "dv", c.n["dv"])
                    c.wait("tensor", "so", c.n["so"])
                    for ci in range(8):
                        c.trans(P2[:, ci * 128:ci * 128 + 128],
                                XTc(ci)[:, t_loc * 128:(t_loc + 1) * 128])
                    c.wait("vector", "mm", c.n["mm"])
                    c.cp(scrF[:, 1024:2048], P2[:, 0:1024])
                    for vc in range(2):
                        pb = SPAIR[vc]
                        c.wait("tensor", "dv", c.n["dv"])
                        for grp in range(3):
                            pitch = C
                            c.wait("gpsimd", "mm", c.n["mm"])
                            tok_w = c.dlg(T["wsA"][:, 0:4096],
                                          bass.AP(T["wd_bf"], grp * 8 * 128 * C + vc * 512,
                                                  [[C, 128], [128 * C, 8], [1, 512]]))
                            c.waitt("tensor", tok_w)
                            for ki in range(8):
                                kk = grp * 8 + ki
                                c.ldw(poolB[:, kk * CHW + tt_ * 128:kk * CHW + tt_ * 128 + 128])
                                c.mm(pb[:, 0:512],
                                     poolB[:, kk * CHW + tt_ * 128:kk * CHW + tt_ * 128 + 128],
                                     T["wsA"][:, ki * 512:ki * 512 + 512],
                                     start=(kk == 0), stop=(kk == 23))
                        c.wait("vector", "mm", c.n["mm"])
                        c.wait("vector", "so", c.n["so"])
                        c.tt(scrF[:, 0:512], pb[:, 0:512],
                             scrF[:, 1024 + vc * 512:1024 + vc * 512 + 512], OP.add)
                        c.wait("sync", "dv", c.n["dv"])
                        c.dso(T["out_shard"][t_loc][:, vc * 512:vc * 512 + 512], scrF[:, 0:512])

    if "ln" not in KSKIP:
        layernorm()
    cc_k = {p: 0 for p in range(NPAIR)}
    cc_v = {p: 0 for p in range(NPAIR)}
    for p in range(NPAIR if "proj" not in KSKIP else 0):
        def kwrite(co, j, psrc, p=p):
            c.wait("vector", "ls", c.n["ls"])
            c.cp(aT[:, 0:CHW], psrc)
            c.wait("sync", "dv", c.n["dv"])
            c.dls(T["kT_loc"][p * 128:(p + 1) * 128, j * CHW:(j + 1) * CHW], aT[:, 0:CHW])
            c.wait("tensor", "ls", c.n["ls"])
        proj_fm(T["wqkv_t"], C + p * 128, 128, kwrite, xn_srcs())
        c.wait("gpsimd", "ls", c.n["ls"])
        c.coll("AllGather", [T["kT_loc"][p * 128:(p + 1) * 128, :].opt()], [T[f"kg{p}"][:, :].opt()])
        cc_k[p] = c.n["cc"]
    for vc in range(2 if "proj" not in KSKIP else 0):
        c.wait("gpsimd", "mm", c.n["mm"])
        tok_w = c.dlg(T["wsA"][:, 0:4096],
                      bass.AP(T["wqkv_t"], 2 * C + vc * 512,
                              [[3 * C, 128], [128 * 3 * C, 8], [1, 512]]))
        c.waitt("tensor", tok_w)
        for tt_ in range(NQT):
            pb = SPAIR[tt_ % 2]
            c.wait("tensor", "dv", c.n["dv"])
            for ki in range(8):
                c.ldw(XN(ki)[:, tt_ * 128:(tt_ + 1) * 128])
                c.mm(pb[:, 0:512], XN(ki)[:, tt_ * 128:(tt_ + 1) * 128],
                     T["wsA"][:, ki * 512:ki * 512 + 512], start=(ki == 0), stop=(ki == 7))
            c.wait("vector", "mm", c.n["mm"])
            c.wait("vector", "ls", c.n["ls"])
            c.cp(aT[:, 0:512], pb[:, 0:512])
            c.wait("sync", "dv", c.n["dv"])
            for pi in range(4):
                p = vc * 4 + pi
                c.dls(T["v_loc"][p, tt_ * 128:(tt_ + 1) * 128, :], aT[:, pi * 128:pi * 128 + 128])
    c.wait("gpsimd", "ls", c.n["ls"])
    for p in range(NPAIR if "proj" not in KSKIP else 0):
        c.coll("AllGather", [T["v_loc"][p, :, :].opt()], [T[f"vg{p}"][:, :].opt()])
        cc_v[p] = c.n["cc"]
    for p in range(NPAIR if "proj" not in KSKIP else 0):
        def qpre(co, p=p):
            c.wait("sync", "dv", c.n["dv"])
            c.dls(T["bias"][:, 0:1], T["bqkv"][p * 128:(p + 1) * 128, :])
        def qwrite(co, j, psrc, p=p):
            c.wait("vector", "ls", c.n["ls"])
            c.wait("vector", "mm", c.n["mm"])
            c.ts(scrF[:, 0:CHW], psrc, T["bias"][:, 0:1], OP.add, float(SCALE_A), OP.mult)
            c.cp(qt[:, p * TOK + j * CHW: p * TOK + j * CHW + CHW], scrF[:, 0:CHW])
            c.wait("tensor", "dv", c.n["dv"])
        proj_fm(T["wqkv_t"], p * 128, 128, qwrite, xn_srcs(), pre_co=qpre)
    c.memset(T["st4"][0:1, 0:8], 0.0)
    for p in range(NPAIR if "stat" not in KSKIP else 0):
        for j in range(QCH):
            c.wait("sync", "dv", c.n["dv"])
            tok_n = c.dls(aT[:, 0:CHW], T["kT_loc"][p * 128:(p + 1) * 128, j * CHW:(j + 1) * CHW])
            c.waitt("vector", tok_n)
            halfsum_max(aT[:, 0:CHW], 0)
    for p in range(NPAIR if "stat" not in KSKIP else 0):
        for j in range(QCH):
            halfsum_max(qt[:, p * TOK + j * CHW: p * TOK + j * CHW + CHW], 1)
    if "stat" not in KSKIP:
        kstat_and_bound(0, 1, 8.0, 1, 2)
    if "att" not in KSKIP:
        attention(True, 1, 2, T["bqkv"], 2 * C, T["woa_t"], T["boa"], cc_k, cc_v)

    if n_sub >= 2:
        ffn(False)

    if n_sub >= 3:
        layernorm()
        def ctw(base, bias_ap):
            def pre(co):
                c.wait("sync", "dv", c.n["dv"])
                c.dls(T["bias"][:, 0:1], bias_ap[co * 128:(co + 1) * 128, :])
            def wr(co, j, psrc):
                c.wait("vector", "ls", c.n["ls"])
                c.wait("vector", "mm", c.n["mm"])
                c.ts(scrF[:, 0:CHW], psrc, T["bias"][:, 0:1], OP.add)
                c.cp(qt[:, (base + co) * TOK + j * CHW: (base + co) * TOK + j * CHW + CHW],
                     scrF[:, 0:CHW])
                c.wait("tensor", "dv", c.n["dv"])
            return pre, wr
        pre, wr = ctw(0, T["bdq"])
        proj_fm(T["wdq"], 0, DQ, wr, xn_srcs(), pre_co=pre)
        pre, wr = ctw(4, T["bdkv"])
        proj_fm(T["wdkv"], 0, DKV, wr, xn_srcs(), pre_co=pre)

        def ct_srcs(base):
            return [(qt, (base + ki) * TOK) for ki in range(4)]

        for ci in range(4):
            for (tag, wR, bR, wC, bC, srcsR, srcsC, scale, dst) in (
                ("q", T["wqr_p"], T["bqr_p"], T["wuq_p"], T["buq_p"],
                 ct_srcs(0), ct_srcs(0), float(SCALE_B), "q_asm"),
                ("k", T["wkr_p"], T["bkr_p"], T["wuk_p"], None,
                 xn_srcs(), ct_srcs(4), None, "kT_loc"),
            ):
                for j in range(QCH):
                    q0 = j * CHW
                    specs = ((wR, srcsR, ci, P0, 0), (wR, srcsR, ci + 4, P0, 512),
                             (wC, srcsC, ci, P1, 0), (wC, srcsC, ci + 4, P1, 512))
                    c.wait("tensor", "dv", c.n["dv"])
                    for ssi, (w_ap, srcs, co, pbk, bofs) in enumerate(specs):
                        tok_w = wload(w_ap, co * 128, 128, len(srcs), dst_off=ssi * 1024)
                        c.waitt("tensor", tok_w)
                        for ki in range(len(srcs)):
                            src, ofs = srcs[ki]
                            c.ldw(T["wsA"][:, ssi * 1024 + ki * 128:ssi * 1024 + (ki + 1) * 128])
                            c.mm(pbk[:, bofs:bofs + CHW],
                                 T["wsA"][:, ssi * 1024 + ki * 128:ssi * 1024 + (ki + 1) * 128],
                                 src[:, ofs + q0:ofs + q0 + CHW],
                                 start=(ki == 0), stop=(ki == len(srcs) - 1))
                    c.wait("sync", "dv", c.n["dv"])
                    tok_b1 = c.dls(T["bias"][:, 1:2], bR[ci * 128:(ci + 1) * 128, :])
                    tok_b2 = c.dls(T["bias"][:, 2:3], bR[(ci + 4) * 128:(ci + 5) * 128, :])
                    if bC is not None:
                        c.dls(T["bias"][:, 5:6], bC[ci * 128:(ci + 1) * 128, :])
                        c.dls(T["bias"][:, 6:7], bC[(ci + 4) * 128:(ci + 5) * 128, :])
                    c.dls(csT[:, 0:CHW], T["cosq"][ci * 128:(ci + 1) * 128, q0:q0 + CHW])
                    tok_cs = c.dls(csT[:, 512:512 + CHW], T["sinq"][ci * 128:(ci + 1) * 128, q0:q0 + CHW])
                    c.wait("vector", "mm", c.n["mm"])
                    c.waitt("vector", tok_cs)
                    c.ts(scrF[:, 0:CHW], P0[:, 0:CHW], T["bias"][:, 1:2], OP.add)
                    c.ts(scrF[:, 512:512 + CHW], P0[:, 512:512 + CHW], T["bias"][:, 2:3], OP.add)
                    if bC is not None:
                        c.ts(scrF[:, 1024:1024 + CHW], P1[:, 0:CHW], T["bias"][:, 5:6], OP.add)
                        c.ts(scrF[:, 1536:1536 + CHW], P1[:, 512:512 + CHW], T["bias"][:, 6:7], OP.add)
                    else:
                        c.cp(scrF[:, 1024:1024 + CHW], P1[:, 0:CHW])
                        c.cp(scrF[:, 1536:1536 + CHW], P1[:, 512:512 + CHW])
                    for comp in range(2):
                        c.tt(scrF[:, 2048:2048 + CHW],
                             scrF[:, (0 if comp == 0 else 512):(0 if comp == 0 else 512) + CHW],
                             csT[:, 0:CHW], OP.mult)
                        c.tt(scrF[:, 2560:2560 + CHW],
                             scrF[:, (512 if comp == 0 else 0):(512 if comp == 0 else 0) + CHW],
                             csT[:, 512:512 + CHW], OP.mult)
                        c.tt(scrF[:, 2048:2048 + CHW], scrF[:, 2048:2048 + CHW],
                             scrF[:, 2560:2560 + CHW], OP.subtract if comp == 0 else OP.add)
                        c.tt(scrF[:, 2048:2048 + CHW], scrF[:, 2048:2048 + CHW],
                             scrF[:, (1024 if comp == 0 else 1536):(1024 if comp == 0 else 1536) + CHW],
                             OP.add)
                        if scale is not None:
                            c.ts(aT[:, comp * 512:comp * 512 + CHW], scrF[:, 2048:2048 + CHW],
                                 scale, OP.mult)
                        else:
                            c.cp(aT[:, comp * 512:comp * 512 + CHW], scrF[:, 2048:2048 + CHW])
                        c.wait("sync", "dv", c.n["dv"])
                        c.dso(bass.AP(T[dst], (256 * ci + 32 * comp) * TOK + q0,
                                      [[64 * TOK, 4], [TOK, 32], [1, CHW]]),
                              aT[:, comp * 512:comp * 512 + CHW])
                    c.wait("vector", "so", c.n["so"])
        for vc in range(2):
            c.wait("gpsimd", "mm", c.n["mm"])
            tok_w = c.dlg(T["wsA"][:, 0:2048],
                          bass.AP(T["wuv"], vc * 512, [[C, 128], [128 * C, 4], [1, 512]]))
            c.waitt("tensor", tok_w)
            for tt_ in range(NQT):
                pb = SPAIR[tt_ % 2]
                c.wait("tensor", "dv", c.n["dv"])
                for ki in range(4):
                    c.ldw(qt[:, (4 + ki) * TOK + tt_ * 128: (4 + ki) * TOK + (tt_ + 1) * 128])
                    c.mm(pb[:, 0:512],
                         qt[:, (4 + ki) * TOK + tt_ * 128: (4 + ki) * TOK + (tt_ + 1) * 128],
                         T["wsA"][:, ki * 512:ki * 512 + 512], start=(ki == 0), stop=(ki == 3))
                c.wait("vector", "mm", c.n["mm"])
                c.wait("vector", "ls", c.n["ls"])
                c.cp(aT[:, 0:512], pb[:, 0:512])
                c.wait("sync", "dv", c.n["dv"])
                for pi in range(4):
                    p = vc * 4 + pi
                    c.dls(T["v_loc"][p, tt_ * 128:(tt_ + 1) * 128, :], aT[:, pi * 128:pi * 128 + 128])
        c.wall("so", c.n["so"])
        tok_q = None
        for p in range(NPAIR):
            c.wait("sync", "mm", c.n["mm"])
            tok_q = c.dls(qt[:, p * TOK:(p + 1) * TOK], T["q_asm"][p * 128:(p + 1) * 128, :])
        c.waitt("tensor", tok_q)
        c.waitt("vector", tok_q)
        c.wait("gpsimd", "ls", c.n["ls"])
        c.wait("gpsimd", "so", c.n["so"])
        for p in range(NPAIR):
            c.coll("AllGather", [T["kT_loc"][p * 128:(p + 1) * 128, :].opt()], [T[f"kg{p}"][:, :].opt()])
            cc_k[p] = c.n["cc"]
            c.coll("AllGather", [T["v_loc"][p, :, :].opt()], [T[f"vg{p}"][:, :].opt()])
            cc_v[p] = c.n["cc"]
        c.memset(T["st4"][0:1, 0:8], 0.0)
        for p in range(NPAIR):
            for j in range(QCH):
                c.wait("sync", "dv", c.n["dv"])
                tok_n = c.dls(aT[:, 0:CHW], T["kT_loc"][p * 128:(p + 1) * 128, j * CHW:(j + 1) * CHW])
                c.waitt("vector", tok_n)
                halfsum_max(aT[:, 0:CHW], 0)
        for p in range(NPAIR):
            for j in range(QCH):
                halfsum_max(qt[:, p * TOK + j * CHW: p * TOK + j * CHW + CHW], 1)
        kstat_and_bound(0, 1, 0.0, 5, 5)
        attention(False, 5, 5, T["buv"], 0, T["wob"], T["bob"], cc_k, cc_v)

    if n_sub >= 4:
        ffn(True)
    elif "fout" in KSKIP:
        for t in range(NQT):
            c.wait("sync", "dv", c.n["dv"])
            c.dso(T["out_shard"][t], xT[:, t * 1024:(t + 1) * 1024] if TOK >= 1024 else xT[:, 0:1024])
    else:
        for t in range(NQT):
            c.wait("tensor", "so", c.n["so"])
            c.wait("tensor", "dv", c.n["dv"])
            for ci in range(8):
                c.trans(P2[:, ci * 128:ci * 128 + 128],
                        XTc(ci)[:, t * 128:(t + 1) * 128])
            c.wait("vector", "mm", c.n["mm"])
            c.wait("vector", "so", c.n["so"])
            c.cp(scrF[:, 0:1024], P2[:, 0:1024])
            c.wait("sync", "dv", c.n["dv"])
            c.dso(T["out_shard"][t], scrF[:, 0:1024])
    c.wait("sync", "so", c.n["so"])


def kernel(**inputs):
    g = geom(S_G)
    cls = classify(g)
    per_core = host_prep(inputs, g)
    nc = build_program(g, cls, N_SUB)
    res = run_bass_kernel_spmd(nc, per_core, core_ids=list(range(8)))
    s_g = g["S"]
    out = np.zeros((B, s_g, C), np.float32)
    for core in range(8):
        b, r = core // 4, core % 4
        o = np.asarray(res.results[core]["out_shard"], np.float32)
        ob = out[b].reshape(s_g // 128, 128, C)
        ob[r::4] = o
    return out
